# version 06
# baseline (speedup 1.0000x reference)
import contextlib
import numpy as np
import ml_dtypes
import concourse.bass as bass
import concourse.mybir as mybir
from concourse.bass_utils import run_bass_kernel_spmd

F32 = mybir.dt.float32
BF16 = mybir.dt.bfloat16
I32 = mybir.dt.int32
AF = mybir.ActivationFunctionType
ALU = mybir.AluOpType
AX = mybir.AxisListType
IOA = bass.IndirectOffsetOnAxis
NPBF = ml_dtypes.bfloat16

T = 4096
C = 1024
NT = 32
DILS = [1, 2, 4, 8]
QOFF = [0, 4096, 6144, 7168]
QLEN = [4096, 2048, 1024, 512]
NE = 8
CAP = 576
KTOP = 1024
HEXP = 4096
AUGC = 1040
HUGE = 1.0e7
NORM = float(1.0 / np.sqrt(128.0))
N_ROUNDS = 4
EPS = 1e-5

ENGS = ("tensor", "vector", "scalar", "sync", "gpsimd")


class Sem:
    def __init__(self, nc, name, owner=None):
        self._cm = nc.semaphore(name)
        self.sem = self._cm.__enter__()
        self.n = 0
        self.owner = owner


class K:

    def __init__(self, nc):
        self.nc = nc
        self.ops = {e: [] for e in ENGS}
        self.sems = {}
        for name, owner in (("qs", None), ("qa", None), ("qg", None),
                            ("pe", "tensor"), ("dv", "vector"), ("ac", "scalar"),
                            ("po", "gpsimd")):
            self.sems[name] = Sem(nc, name, owner)
        self.gate = []

    def set_gate(self):
        self.gate = [(s, s.n) for s in self.sems.values()]

    def op(self, eng, fn, waits=(), inc=None, k=1):
        w = [(s, c) for (s, c) in list(self.gate) + list(waits) if s is not None and c]
        self.ops[eng].append((w, fn, inc, k))
        if inc is not None:
            inc.n += k
            return inc.n
        return None

    def emit(self, block):
        for eng in ENGS:
            oplist = self.ops[eng]
            if not oplist:
                continue

            def make(oplist=oplist, eng=eng):
                def body(e):
                    waited = {}
                    for w, fn, inc, kkk in oplist:
                        for s, c in w:
                            if s.owner == eng:
                                continue
                            if waited.get(id(s), -1) >= c:
                                continue
                            e.wait_ge(s.sem, c)
                            waited[id(s)] = c
                        ins = fn(e)
                        if inc is not None:
                            if ins is None:
                                ins = e.nop()
                            ins.then_inc(inc.sem, kkk)
                return body
            getattr(block, eng)(make())


def _mask_const():
    m = np.zeros((4, 128, 128), np.float32)
    for h, dil in enumerate(DILS):
        L = 32 // dil
        G = 128 // L
        n_i = np.repeat(np.arange(L), G)
        s_i = np.tile(np.arange(G), L)
        valid = (s_i[:, None] == s_i[None, :]) & (n_i[:, None] >= n_i[None, :])
        m[h] = np.where(valid, 0.0, -1e30).astype(np.float32)
    return m


def make_consts(g):
    c = {}
    c["ident_bf"] = np.eye(128).astype(NPBF)
    c["ident_f32"] = np.eye(128).astype(np.float32)
    c["masks"] = np.ascontiguousarray(_mask_const().transpose(1, 0, 2).reshape(128, 512))
    i = np.arange(128)
    c["trilU_f32"] = (i[:, None] <= i[None, :]).astype(np.float32)
    blk = (i[:, None] // 32 == i[None, :] // 32)
    c["tril32_f32"] = (blk & (i[:, None] < i[None, :])).astype(np.float32)
    c["ones_row_bf"] = np.ones((1, 128), NPBF)
    c["ones_row_f32"] = np.ones((1, 128), np.float32)
    c["ones_col_f32"] = np.ones((128, 1), np.float32)
    c["ones_col_bf"] = np.ones((128, 1), NPBF)
    c["grid_col"] = (np.arange(1, 129, dtype=np.float32) / 128.0).reshape(128, 1)
    hm = np.zeros((128, 32), np.float32)
    hm[:, 16 * g:16 * (g + 1)] = 1.0
    c["halfmask"] = hm
    idc = np.zeros((128, 64), np.float32)
    for t in range(32):
        tok = t * 128 + np.arange(128)
        idc[:, 2 * t] = tok // 32
        idc[:, 2 * t + 1] = tok % 32
    c["idcols"] = idc.astype(NPBF)
    c["epscol"] = np.full((128, 1), EPS, np.float32)
    return c


CONST_SPECS = [
    ("ident_bf", [128, 128], BF16), ("ident_f32", [128, 128], F32),
    ("masks", [128, 512], F32), ("trilU_f32", [128, 128], F32),
    ("tril32_f32", [128, 128], F32), ("ones_row_bf", [1, 128], BF16),
    ("ones_row_f32", [1, 128], F32), ("ones_col_f32", [128, 1], F32),
    ("ones_col_bf", [128, 1], BF16), ("grid_col", [128, 1], F32),
    ("halfmask", [128, 32], F32), ("idcols", [128, 64], BF16),
    ("epscol", [128, 1], F32),
]


def build_nc(phases=5, debug=False):
    nc = bass.Bass("TRN2", target_bir_lowering=False)

    D = {}
    D["x"] = nc.dram_tensor("x", [T, C], F32, kind="ExternalInput").ap()
    D["wk"] = nc.dram_tensor("wk", [4, 128, C], F32, kind="ExternalInput").ap()
    D["wq"] = nc.dram_tensor("wq", [4, 128, C], F32, kind="ExternalInput").ap()
    D["wv"] = nc.dram_tensor("wv", [4, 128, C], F32, kind="ExternalInput").ap()
    D["wout"] = nc.dram_tensor("wout", [C, 512], F32, kind="ExternalInput").ap()
    D["bout"] = nc.dram_tensor("bout", [1, C], F32, kind="ExternalInput").ap()
    D["gatew"] = nc.dram_tensor("gatew", [C, NE], F32, kind="ExternalInput").ap()
    D["w1"] = nc.dram_tensor("w1", [NE, C, HEXP], F32, kind="ExternalInput").ap()
    D["w2"] = nc.dram_tensor("w2", [NE, HEXP, C], F32, kind="ExternalInput").ap()
    for nm, shp, dt in CONST_SPECS:
        D[nm] = nc.dram_tensor(nm, shp, dt, kind="ExternalInput").ap()

    D["out"] = nc.dram_tensor("out", [T, C], F32, kind="ExternalOutput").ap()
    dbg = "ExternalOutput" if debug else "Internal"
    D["y1d"] = nc.dram_tensor("y1d", [T, C], BF16, kind=dbg).ap()
    D["probsd"] = nc.dram_tensor("probsd", [NE, T], F32, kind=dbg).ap()
    D["thrd"] = nc.dram_tensor("thrd", [1, 24], F32, kind=dbg).ap()
    D["invd"] = nc.dram_tensor("invd", [128, 256], I32, kind=dbg).ap()
    for _e in range(NE):
        D[f"xind{_e}"] = nc.dram_tensor(f"xind{_e}", [640, AUGC], BF16, kind=dbg).ap()
    D["accd"] = nc.dram_tensor("accd", [T, C], F32, kind=dbg).ap()
    D["qd"] = nc.dram_tensor("qd", [128, 4096], BF16, kind=dbg).ap()
    D["kd"] = nc.dram_tensor("kd", [128, 4096], BF16, kind=dbg).ap()
    D["vd"] = nc.dram_tensor("vd", [128, 4096], BF16, kind=dbg).ap()
    D["catd"] = nc.dram_tensor("catd", [128, 4 * T], BF16, kind=dbg).ap()
    D["scd"] = nc.dram_tensor("scd", [128, 128], F32, kind=dbg).ap()
    D["expd"] = nc.dram_tensor("expd", [128, 128], F32, kind=dbg).ap()
    D["smd"] = nc.dram_tensor("smd", [128, 128], F32, kind=dbg).ap()
    D["ard"] = nc.dram_tensor("ard", [128, 16], F32, kind=dbg).ap()

    ES = contextlib.ExitStack()
    def _sb(n, shp, dt):
        return ES.enter_context(nc.sbuf_tensor("sb_"+"sb_" + n, shp, dt))
    def _ps(n, shp, dt):
        return ES.enter_context(nc.psum_tensor(n, shp, dt))
    ident_bf = _sb("ident_bf", [128, 128], BF16)
    ident_f32 = _sb("ident_f32", [128, 128], F32)
    masks = _sb("masks", [128, 512], F32)
    trilU = _sb("trilU", [128, 128], F32)
    tril32 = _sb("tril32", [128, 128], F32)
    ones_row_bf = _sb("ones_row_bf", [1, 128], BF16)
    ones_row_f32 = _sb("ones_row_f32", [1, 128], F32)
    ones_col_f32 = _sb("ones_col_f32", [128, 1], F32)
    ones_col_bf = _sb("ones_col_bf", [128, 1], BF16)
    grid_col = _sb("grid_col", [128, 1], F32)
    grid_row = _sb("grid_row", [1, 128], F32)
    grid8 = _sb("grid8", [8, 128], F32)
    halfmask = _sb("halfmask", [128, 32], F32)
    idcols = _sb("idcols", [128, 64], BF16)
    boutbf = _sb("boutbf", [1, C], BF16)
    gatewbf = _sb("gatewbf", [128, 64], BF16)
    gsmall = _sb("gsmall", [128, 32], BF16)
    growsc = _sb("growsc", [1, 64], F32)
    gbcast = _sb("gbcast", [128, NE], F32)
    bgb = _sb("bgb", [128, NE], F32)
    probs_tok = _sb("probs_tok", [128, 256], F32)
    probs_ft = _sb("probs_ft", [8, 2 * 128], F32)
    inv_i32 = _sb("inv_i32", [128, 256], I32)
    lncol = _sb("lncol", [128, 32], F32)
    sqscr = _sb("sqscr", [128, C], BF16)
    epscol = _sb("epscol", [128, 1], F32)
    sp1 = _sb("sp1", [128, 64], F32)
    sp2 = _sb("sp2", [128, 64], F32)
    dbg1 = _sb("dbg1", [128, 128], F32)
    dbg2 = _sb("dbg2", [128, 128], F32)
    dbg3 = _sb("dbg3", [128, 128], F32)
    dbg4 = _sb("dbg4", [128, 16], F32)
    pA = _ps("pA", [128, 512], F32)
    pB = _ps("pB", [128, 512], F32)
    pC = _ps("pC", [128, 512], F32)
    pD = _ps("pD", [128, 512], F32)
    pS = _ps("pS", [128, 128], F32)
    pS2 = _ps("pS2", [128, 128], F32)
    pT = _ps("pT", [128, 128], BF16)
    pT2 = _ps("pT2", [128, 128], BF16)
    with nc.Block() as block:

        kk = K(nc)
        qs, qa, qg = kk.sems["qs"], kk.sems["qa"], kk.sems["qg"]
        pe, dv, ac = kk.sems["pe"], kk.sems["dv"], kk.sems["ac"]

        PS = {"pA": pA, "pB": pB, "pC": pC, "pD": pD, "pS": pS, "pS2": pS2,
              "pT": pT, "pT2": pT2}
        free = {k2: None for k2 in PS}

        def SPV():
            kk.op("vector", lambda e: e.tensor_copy(sp1[:], masks[:, 0:64]))

        def SPA():
            kk.op("scalar", lambda e: e.activation(sp2[:], masks[:, 64:128], AF.Copy))

        def mm_group(psn, out_ap, pairs, waits, cons_eng, cons_fn, cons_sem):
            def f(e, pairs=pairs, out_ap=out_ap):
                n = len(pairs)
                ins = None
                for idx, (lh, rh) in enumerate(pairs):
                    ins = e.matmul(out_ap, lh, rh, start=(idx == 0), stop=(idx == n - 1))
                return ins
            w = list(waits)
            if free[psn] is not None:
                w.append(free[psn])
            mmk = kk.op("tensor", f, waits=w, inc=pe)
            cmk = kk.op(cons_eng, cons_fn, waits=[(pe, mmk)], inc=cons_sem)
            free[psn] = (cons_sem, cmk)
            return cmk

        def transp(psn, src_ap, dst_fn, cons_sem, waits=(), ident=None, tpos=None):
            p = PS[psn]
            ident = ident if ident is not None else ident_bf
            w = list(waits)
            if free[psn] is not None:
                w.append(free[psn])
            oap = p[:] if tpos is None else tpos
            mmk = kk.op("tensor", lambda e, s=src_ap, oap=oap, ident=ident: e.transpose(oap, s, ident[:]),
                        waits=w, inc=pe)
            cmk = kk.op(cons_sem[0], lambda e, oap=oap: dst_fn(e, oap), waits=[(pe, mmk)], inc=cons_sem[1])
            free[psn] = (cons_sem[1], cmk)
            return cmk

        ESA = contextlib.ExitStack()
        xfeat = ESA.enter_context(nc.sbuf_tensor("sb_"+"xfeat", [128, 8 * T], BF16))
        cat4 = ESA.enter_context(nc.sbuf_tensor("sb_"+"cat4", [128, 4 * T], BF16))
        wqT = ESA.enter_context(nc.sbuf_tensor("sb_"+"wqT", [128, 4096], BF16))
        wkT = ESA.enter_context(nc.sbuf_tensor("sb_"+"wkT", [128, 4096], BF16))
        wvT = ESA.enter_context(nc.sbuf_tensor("sb_"+"wvT", [128, 4096], BF16))
        woutT = ESA.enter_context(nc.sbuf_tensor("sb_"+"woutT", [128, 4096], BF16))
        xt32 = ESA.enter_context(nc.sbuf_tensor("sb_"+"xt32", [128, 2 * C], F32))
        xtbf = ESA.enter_context(nc.sbuf_tensor("sb_"+"xtbf", [128, 2 * C], BF16))
        if True:
            def xf(cc):
                return xfeat[:, cc * T:(cc + 1) * T]

            def cath(h):
                return cat4[:, h * T:(h + 1) * T]

            for nm, tile in (("ident_bf", ident_bf), ("ident_f32", ident_f32),
                             ("masks", masks), ("trilU_f32", trilU), ("tril32_f32", tril32),
                             ("ones_row_bf", ones_row_bf), ("ones_row_f32", ones_row_f32),
                             ("ones_col_f32", ones_col_f32), ("ones_col_bf", ones_col_bf),
                             ("grid_col", grid_col), ("halfmask", halfmask), ("idcols", idcols),
                             ("epscol", epscol)):
                kk.op("sync", lambda e, t=tile, n=nm: e.dma_start(out=t[:], in_=D[n]), inc=qs, k=16)
            m_consts = qs.n

            ESW = contextlib.ExitStack()
            wqbf = ESW.enter_context(nc.sbuf_tensor("sb_"+"wqbf", [128, 4096], BF16))
            wkbf = ESW.enter_context(nc.sbuf_tensor("sb_"+"wkbf", [128, 4096], BF16))
            wvbf = ESW.enter_context(nc.sbuf_tensor("sb_"+"wvbf", [128, 4096], BF16))
            woutbf = ESW.enter_context(nc.sbuf_tensor("sb_"+"woutbf", [128, 4096], BF16))
            zero4k = ESW.enter_context(nc.sbuf_tensor("sb_"+"zero4k", [128, 2048], F32))
            bcol = ESW.enter_context(nc.sbuf_tensor("sb_"+"bcol", [128, 8], BF16))
            if True:
                kk.op("gpsimd", lambda e: e.dma_start(out=wqbf[:].rearrange("p (h c) -> p h c", c=C), in_=D["wq"].rearrange("h p c -> p h c")), inc=qg, k=16)
                kk.op("gpsimd", lambda e: e.dma_start(out=wkbf[:].rearrange("p (h c) -> p h c", c=C), in_=D["wk"].rearrange("h p c -> p h c")), inc=qg, k=16)
                kk.op("gpsimd", lambda e: e.dma_start(out=wvbf[:].rearrange("p (h c) -> p h c", c=C), in_=D["wv"].rearrange("h p c -> p h c")), inc=qg, k=16)
                kk.op("gpsimd", lambda e: e.dma_start(out=woutbf[:].rearrange("p (cc v) -> p cc v", v=512), in_=D["wout"].rearrange("(cc p) v -> p cc v", p=128)), inc=qg, k=16)
                kk.op("gpsimd", lambda e: e.dma_start(out=boutbf[:], in_=D["bout"]), inc=qg, k=16)
                kk.op("gpsimd", lambda e: e.dma_start(out=gatewbf[:].rearrange("p (cc e) -> p cc e", e=NE), in_=D["gatew"].rearrange("(cc p) e -> p cc e", p=128)), inc=qg, k=16)
                m_wbf = qg.n

                m_z4k = kk.op("vector", lambda e: e.memset(zero4k[:], 0.0), inc=dv)
                accf = D["accd"].rearrange("a b -> (a b)").rearrange("(p f) -> p f", p=128)
                for i in range(16):
                    kk.op("sync", lambda e, i=i: e.dma_start(out=accf[:, i * 2048:(i + 1) * 2048], in_=zero4k[:]),
                          waits=[(dv, m_z4k)], inc=qs, k=16)
                zb = zero4k[:].bitcast(BF16)
                for _e in range(NE):
                    xinf_flat = D[f"xind{_e}"].rearrange("r c -> (r c)").rearrange("(p f) -> p f", p=128)
                    nf = xinf_flat.shape[1]
                    i = 0
                    while i < nf:
                        wdt = min(4096, nf - i)
                        kk.op("sync", lambda e, _e=_e, i=i, wdt=wdt, xinf_flat=xinf_flat: e.dma_start(
                            out=xinf_flat[:, i:i + wdt], in_=zb[:, :wdt]),
                            waits=[(dv, m_z4k)], inc=qs, k=16)
                        i += wdt
                m_zfill = qs.n

                m_grow_ = mm_group("pS", pS[0:1, 0:128], [(grid_col[:], ident_f32[:])],
                         [(qs, m_consts)], "vector",
                         lambda e: e.tensor_copy(grid_row[:], pS[0:1, 0:128]), dv)
                mm_group("pS", pS[0:8, 0:128], [(ones_row_f32[0:1, 0:8], grid_row[:])],
                         [(dv, m_grow_)], "vector",
                         lambda e: e.tensor_copy(grid8[:], pS[0:8, 0:128]), dv)

                tjobs = []
                for wt, wbf in ((wqT, wqbf), (wkT, wkbf), (wvT, wvbf)):
                    for h in range(4):
                        for cc in range(8):
                            tjobs.append((wbf[:, h * C + cc * 128: h * C + cc * 128 + 128],
                                          wt[:, (h * 8 + cc) * 128:(h * 8 + cc) * 128 + 128]))
                for h in range(4):
                    for cc in range(8):
                        tjobs.append((woutbf[:, cc * 512 + h * 128: cc * 512 + h * 128 + 128],
                                      woutT[:, h * C + cc * 128: h * C + cc * 128 + 128]))
                for i, (src, dst) in enumerate(tjobs):
                    psn = "pT" if i % 2 == 0 else "pT2"
                    transp(psn, src, lambda e, oap, d=dst: e.tensor_copy(d, oap),
                           ("vector", dv), waits=[(qg, m_wbf), (qs, m_consts)])

                for hv in range(4):
                    pairs = [(woutbf[:, cc * 512 + hv * 128: cc * 512 + hv * 128 + 128],
                              gatewbf[:, cc * 8:(cc + 1) * 8]) for cc in range(8)]
                    mm_group("pS", pS[:, 0:8], pairs, [(qg, m_wbf)], "vector",
                             lambda e, hv=hv: e.tensor_copy(gsmall[:, hv * 8:(hv + 1) * 8], pS[:, 0:8]), dv)

                for cc in range(8):
                    mm_group("pS", pS[0:128, 0:1],
                             [(boutbf[0:1, cc * 128:(cc + 1) * 128], ones_row_bf[0:1, 0:1])],
                             [(qg, m_wbf), (qs, m_consts)], "vector",
                             lambda e, cc=cc: e.tensor_copy(bcol[:, cc:cc + 1], pS[0:128, 0:1]), dv)
                m_bcol = dv.n
                pairs = [(bcol[:, cc:cc + 1], gatewbf[:, cc * 8:(cc + 1) * 8]) for cc in range(8)]
                mm_group("pS", pS[0:1, 0:8], pairs, [(dv, m_bcol)], "vector",
                         lambda e: e.tensor_copy(growsc[0:1, 0:8], pS[0:1, 0:8]), dv)
                pairs = [(gatewbf[:, cc * 8:(cc + 1) * 8], ones_col_bf[:]) for cc in range(8)]
                mG = mm_group("pS", pS[0:8, 0:1], pairs, [(qg, m_wbf), (qs, m_consts)], "vector",
                              lambda e: e.tensor_copy(lncol[0:8, 30:31], pS[0:8, 0:1]), dv)
                mm_group("pS", pS[0:1, 0:8], [(lncol[0:8, 30:31], ident_f32[0:8, 0:8])],
                         [(dv, mG)], "vector",
                         lambda e: e.tensor_copy(growsc[0:1, 8:16], pS[0:1, 0:8]), dv)
                m_grow = dv.n
                mm_group("pS", pS[:, 0:8], [(ones_row_f32[:], growsc[0:1, 8:16])], [(dv, m_grow)],
                         "vector", lambda e: e.tensor_copy(gbcast[:], pS[:, 0:8]), dv)
                mm_group("pS", pS[:, 0:8], [(ones_row_f32[:], growsc[0:1, 0:8])], [(dv, m_grow)],
                         "vector", lambda e: e.tensor_copy(bgb[:], pS[:, 0:8]), dv)
                m_setup = dv.n

                xload = [0] * NT
                xcast = [0] * NT
                xTev = [0] * NT
                for k in range(NT):
                    sl = k % 2
                    w = []
                    if k >= 2:
                        w.append((ac, xcast[k - 2]))
                    xload[k] = kk.op("sync", lambda e, k=k, sl=sl: e.dma_start(
                        out=xt32[:, sl * C:sl * C + C], in_=D["x"][k * 128:(k + 1) * 128, :]),
                        waits=w, inc=qs, k=16)
                    w = [(qs, xload[k])]
                    if k >= 2:
                        w.append((dv, xTev[k - 2]))
                    xcast[k] = kk.op("scalar", lambda e, sl=sl: e.activation(
                        xtbf[:, sl * C:sl * C + C], xt32[:, sl * C:sl * C + C], AF.Copy),
                        waits=w, inc=ac)
                    last = 0
                    for cc in range(8):
                        psn = "pT" if (k * 8 + cc) % 2 == 0 else "pT2"
                        last = transp(psn, xtbf[:, sl * C + cc * 128:sl * C + cc * 128 + 128],
                                      lambda e, oap, cc=cc, k=k: e.tensor_copy(
                                          xf(cc)[:, k * 128:(k + 1) * 128], oap),
                                      ("vector", dv), waits=[(ac, xcast[k]), (qs, m_consts)])
                    xTev[k] = last
                m_xfeat = dv.n
            ESW.close()
            kk.set_gate()

            m_catz = kk.op("vector", lambda e: e.memset(cat4[:], 0.0), inc=dv)

            ESH = contextlib.ExitStack()
            qh = ESH.enter_context(nc.sbuf_tensor("sb_"+"qh", [128, 4096], BF16))
            khh = ESH.enter_context(nc.sbuf_tensor("sb_"+"kh", [128, 4096], BF16))
            vh = ESH.enter_context(nc.sbuf_tensor("sb_"+"vh", [128, 4096], BF16))
            scoreS = ESH.enter_context(nc.sbuf_tensor("sb_"+"scoreS", [128, 2 * 128], F32))
            smbf = ESH.enter_context(nc.sbuf_tensor("sb_"+"smbf", [128, 2 * 128], BF16))
            attbf = ESH.enter_context(nc.sbuf_tensor("sb_"+"attbf", [128, 2 * 128], BF16))
            arow = ESH.enter_context(nc.sbuf_tensor("sb_"+"arow", [128, 16], F32))
            vtb = ESH.enter_context(nc.sbuf_tensor("sb_vtb", [128, 2 * 128], BF16))
            if True:
                for h in range(4):
                    dil = DILS[h]
                    L = 32 // dil
                    G = 128 // L
                    for dst, wT in ((qh, wqT), (khh, wkT), (vh, wvT)):
                        ngroups = QLEN[h] // 512
                        nblk = 32 // dil
                        G = 128 // nblk
                        dst_r = dst[:, 0:QLEN[h]].rearrange(
                            "p (gi n sj) -> p gi n sj", gi=nblk, n=nblk, sj=G)
                        for grp in range(ngroups):
                            psn = "pA" if grp % 2 == 0 else "pB"
                            ps = PS[psn]

                            def fqk(e, h=h, dil=dil, grp=grp, wT=wT, ps=ps):
                                ins = None
                                if dil == 1:
                                    for cc in range(8):
                                        ins = e.matmul(ps[:],
                                                       wT[:, (h * 8 + cc) * 128:(h * 8 + cc) * 128 + 128],
                                                       xf(cc)[:, grp * 512:(grp + 1) * 512],
                                                       start=(cc == 0), stop=(cc == 7))
                                else:
                                    for j in range(4):
                                        n = grp * 4 + j
                                        t0 = n * dil * 128
                                        for cc in range(8):
                                            ins = e.matmul(ps[:, j * 128:(j + 1) * 128],
                                                           wT[:, (h * 8 + cc) * 128:(h * 8 + cc) * 128 + 128],
                                                           xf(cc)[:, t0:t0 + 128],
                                                           start=(cc == 0), stop=(cc == 7))
                                return ins
                            w = [(dv, m_xfeat)]
                            if free[psn] is not None:
                                w.append(free[psn])
                            mmk = kk.op("tensor", fqk, waits=w, inc=pe)
                            ps_r = ps[:].rearrange("p (n gi sj) -> p n gi sj", n=4, gi=nblk, sj=G)
                            src = ps_r.transpose([0, 2, 1, 3])
                            dslc = dst_r[:, :, grp * 4:grp * 4 + 4, :]
                            cmk = kk.op("vector", lambda e, dslc=dslc, src=src: e.tensor_copy(dslc, src),
                                        waits=[(pe, mmk)], inc=dv)
                            free[psn] = (dv, cmk)
                    m_qkvh = dv.n

                    if debug and h == 0:
                        kk.op("scalar", lambda e: e.dma_start(out=D["qd"], in_=qh[:]),
                              waits=[(dv, m_qkvh)], inc=qa, k=16)
                        kk.op("scalar", lambda e: e.dma_start(out=D["kd"], in_=khh[:]),
                              waits=[(dv, m_qkvh)], inc=qa, k=16)
                        kk.op("scalar", lambda e: e.dma_start(out=D["vd"], in_=vh[:]),
                              waits=[(dv, m_qkvh)], inc=qa, k=16)
                    for gi in range(L):
                        sl = gi % 2
                        psn = "pS" if sl == 0 else "pS2"
                        ptn = "pT" if sl == 0 else "pT2"
                        ps = PS[psn]
                        s0 = gi * G
                        sc = scoreS[:, sl * 128:sl * 128 + 128]
                        sm = smbf[:, sl * 128:sl * 128 + 128]
                        ab = attbf[:, sl * 128:sl * 128 + 128]
                        vt = vtb[:, sl * 128:sl * 128 + 128]
                        ar = arow[:, sl * 8:sl * 8 + 8]
                        lhsT = khh[:, gi * 128:(gi + 1) * 128]
                        rhs = qh[:, gi * 128:(gi + 1) * 128]
                        mvt = transp(ptn, vh[:, gi * 128:(gi + 1) * 128],
                                     lambda e, oap, vt=vt: e.tensor_copy(vt, oap),
                                     ("vector", dv), waits=[(dv, m_qkvh)])
                        w = [(dv, m_qkvh)]
                        if free[psn] is not None:
                            w.append(free[psn])
                        mm = kk.op("tensor", lambda e, ps=ps, lhsT=lhsT, rhs=rhs: e.matmul(
                            ps[:], lhsT, rhs, start=True, stop=True), waits=w, inc=pe)
                        mv = kk.op("vector", lambda e, ps=ps, sc=sc, h=h: e.tensor_add(
                            sc, ps[:], masks[:, h * 128:(h + 1) * 128]), waits=[(pe, mm)], inc=dv)
                        free[psn] = (dv, mv)
                        if debug and h == 0 and gi == 0:
                            kk.op("vector", lambda e, sc=sc: e.tensor_copy(dbg1[:], sc),
                                  waits=[(dv, mv)], inc=dv)
                        mv = kk.op("vector", lambda e, sc=sc, ar=ar: e.reduce_max(
                            ar[:, 0:1], sc, axis=AX.X, negate=True), waits=[(dv, mv)], inc=dv)
                        SPV()
                        mv = kk.op("vector", lambda e, sc=sc, ar=ar: e.tensor_scalar(
                            sc, sc, ar[:, 0:1], NORM, ALU.add, ALU.mult), waits=[(dv, mv)], inc=dv)
                        ma = kk.op("scalar", lambda e, sc=sc: e.activation(
                            sc, sc, AF.Exp), waits=[(dv, mv)], inc=ac)
                        if debug and h == 0 and gi == 0:
                            kk.op("vector", lambda e, sc=sc: e.tensor_copy(dbg2[:], sc),
                                  waits=[(ac, ma)], inc=dv)
                        mv = kk.op("vector", lambda e, sc=sc, ar=ar: e.reduce_sum(
                            ar[:, 2:3], sc, axis=AX.X), waits=[(ac, ma)], inc=dv)
                        SPV()
                        mv = kk.op("vector", lambda e, ar=ar: e.reciprocal(
                            ar[:, 3:4], ar[:, 2:3]), waits=[(dv, mv)], inc=dv)
                        SPV()
                        mv = kk.op("vector", lambda e, sc=sc, sm=sm, ar=ar: e.tensor_scalar(
                            sm, sc, ar[:, 3:4], None, ALU.mult), waits=[(dv, mv)], inc=dv)
                        if debug and h == 0 and gi == 0:
                            kk.op("vector", lambda e, sm=sm: e.tensor_copy(dbg3[:], sm),
                                  waits=[(dv, mv)], inc=dv)
                            kk.op("vector", lambda e, ar=ar: e.tensor_copy(dbg4[:, 0:8], ar),
                                  waits=[(dv, mv)], inc=dv)
                        w = [(dv, mv), (dv, mvt)]
                        if free[psn] is not None:
                            w.append(free[psn])
                        mm = kk.op("tensor", lambda e, ps=ps, sm=sm, vt=vt: e.matmul(
                            ps[:], sm, vt, start=True, stop=True),
                            waits=w, inc=pe)
                        ma = kk.op("scalar", lambda e, ps=ps, ab=ab: e.activation(
                            ab, ps[:], AF.Copy), waits=[(pe, mm)], inc=ac)
                        free[psn] = (ac, ma)
                        catv = cath(h).rearrange("v (s q) -> v q s", q=32)
                        dest = catv[:, 0:32:dil, s0:s0 + G]
                        transp(ptn, ab, lambda e, oap, dest=dest, G=G: e.tensor_copy(
                            dest, oap.rearrange("v (m s) -> v m s", s=G)),
                            ("vector", dv), waits=[(ac, ma), (dv, m_catz)])
            ESH.close()
            if debug:
                kk.op("scalar", lambda e: e.dma_start(out=D["catd"], in_=cat4[:]),
                      waits=[(dv, dv.n), (ac, ac.n)], inc=qa, k=16)
                kk.op("scalar", lambda e: e.dma_start(out=D["scd"], in_=dbg1[:]), inc=qa, k=16)
                kk.op("scalar", lambda e: e.dma_start(out=D["expd"], in_=dbg2[:]), inc=qa, k=16)
                kk.op("scalar", lambda e: e.dma_start(out=D["smd"], in_=dbg3[:]), inc=qa, k=16)
                kk.op("scalar", lambda e: e.dma_start(out=D["ard"], in_=dbg4[:]), inc=qa, k=16)
            ESH.close()
            kk.set_gate()
            m_att = dv.n

            y1_store = [0] * NT
            if phases >= 2:
                ESB = contextlib.ExitStack()
                zt = ESB.enter_context(nc.sbuf_tensor("sb_"+"zt", [128, 2 * C], F32))
                y1bf = ESB.enter_context(nc.sbuf_tensor("sb_"+"y1bf", [128, 2 * C], BF16))
                gsc = ESB.enter_context(nc.sbuf_tensor("sb_"+"gsc", [128, 2 * 16], F32))
                if True:
                    lnfree = [None, None]
                    zfree = [None, None]
                    gfree = [None, None]
                    xfree = [None, None]
                    pf_store = [0] * NT
                    for k in range(NT):
                        sl = k % 2
                        ln0 = sl * 12
                        psn1 = "pA" if sl == 0 else "pC"
                        psn2 = "pB" if sl == 0 else "pD"
                        psg = "pS" if sl == 0 else "pS2"
                        xsl = xt32[:, sl * C:sl * C + C]
                        zsl = zt[:, sl * C:sl * C + C]
                        y1sl = y1bf[:, sl * C:sl * C + C]
                        gs = gsc[:, sl * 16:sl * 16 + 16]

                        w = [(dv, m_att)]
                        if xfree[sl] is not None:
                            w.append(xfree[sl])
                        xl = kk.op("sync", lambda e, k=k, xsl=xsl: e.dma_start(
                            out=xsl, in_=D["x"][k * 128:(k + 1) * 128, :]), waits=w, inc=qs, k=16)

                        zmks = []
                        for half, psn in ((0, psn1), (1, psn2)):
                            ps = PS[psn]
                            pairs = []
                            for hv in range(4):
                                pairs.append((cath(hv)[:, k * 128:(k + 1) * 128],
                                              woutT[:, hv * C + half * 512: hv * C + half * 512 + 512]))
                            pairs.append((ones_row_bf[:], boutbf[:, half * 512:half * 512 + 512]))
                            cmk = mm_group(psn, ps[:], pairs, [(dv, m_att), (qs, xl)], "vector",
                                           lambda e, ps=ps, zsl=zsl, xsl=xsl, half=half: e.tensor_add(
                                               zsl[:, half * 512:half * 512 + 512], ps[:],
                                               xsl[:, half * 512:half * 512 + 512]), dv)
                            zmks.append(cmk)
                        m_z = zmks[-1]
                        xfree[sl] = (dv, m_z)

                        pairs = []
                        for hv in range(4):
                            pairs.append((cath(hv)[:, k * 128:(k + 1) * 128], gsmall[:, hv * 8:(hv + 1) * 8]))
                        for cc in range(8):
                            pairs.append((xf(cc)[:, k * 128:(k + 1) * 128], gatewbf[:, cc * 8:(cc + 1) * 8]))
                        psgp = PS[psg]
                        def fg(e, pairs=pairs, psgp=psgp):
                            n = len(pairs)
                            for idx, (lh, rh) in enumerate(pairs):
                                ins = e.matmul(psgp[:, 0:8], lh, rh, start=(idx == 0), stop=(idx == n - 1))
                            return ins
                        w = [(dv, m_att), (dv, m_setup)]
                        if free[psg] is not None:
                            w.append(free[psg])
                        mmg = kk.op("tensor", fg, waits=w, inc=pe)

                        w = [(dv, m_z)]
                        if lnfree[sl] is not None:
                            w.append(lnfree[sl])
                        mv = kk.op("vector", lambda e, zsl=zsl, ln0=ln0: e.reduce_sum(
                            lncol[:, ln0:ln0 + 1], zsl, axis=AX.X, negate=True), waits=w, inc=dv)
                        SPV()
                        mv = kk.op("vector", lambda e, ln0=ln0: e.tensor_scalar_mul(
                            lncol[:, ln0 + 1:ln0 + 2], lncol[:, ln0:ln0 + 1], 1.0 / C), waits=[(dv, mv)], inc=dv)
                        mv2 = kk.op("vector", lambda e, zsl=zsl, ln0=ln0: e.scalar_tensor_tensor(
                            sqscr[:], zsl, 1.0, zsl, ALU.mult, ALU.mult, accum_out=lncol[:, ln0 + 2:ln0 + 3]),
                            waits=[(dv, mv)], inc=dv)
                        mv3 = kk.op("vector", lambda e, ln0=ln0: e.tensor_tensor(
                            lncol[:, ln0 + 3:ln0 + 4], lncol[:, ln0 + 1:ln0 + 2], lncol[:, ln0 + 1:ln0 + 2],
                            ALU.mult), waits=[(dv, mv2)], inc=dv)
                        mv3 = kk.op("vector", lambda e, ln0=ln0: e.tensor_scalar(
                            lncol[:, ln0 + 4:ln0 + 5], lncol[:, ln0 + 2:ln0 + 3], 1.0 / C, None, ALU.mult),
                            waits=[(dv, mv3)], inc=dv)
                        SPV()
                        mv3 = kk.op("vector", lambda e, ln0=ln0: e.tensor_sub(
                            lncol[:, ln0 + 5:ln0 + 6], lncol[:, ln0 + 4:ln0 + 5], lncol[:, ln0 + 3:ln0 + 4]),
                            waits=[(dv, mv3)], inc=dv)
                        ma = kk.op("scalar", lambda e, ln0=ln0: e.activation(
                            lncol[:, ln0 + 6:ln0 + 7], lncol[:, ln0 + 5:ln0 + 6], AF.Ln, bias=epscol[:]),
                            waits=[(dv, mv3)], inc=ac)
                        SPA()
                        ma = kk.op("scalar", lambda e, ln0=ln0: e.activation(
                            lncol[:, ln0 + 7:ln0 + 8], lncol[:, ln0 + 6:ln0 + 7], AF.Exp, scale=-0.5),
                            waits=[(ac, ma)], inc=ac)
                        mv4 = kk.op("vector", lambda e, ln0=ln0: e.tensor_copy(
                            lncol[:, ln0 + 3:ln0 + 4], lncol[:, ln0 + 7:ln0 + 8]), waits=[(ac, ma)], inc=dv)
                        SPV()
                        w_y1 = [(dv, mv4)]
                        if k >= 2 and y1_store[k - 2]:
                            w_y1.append((qa, y1_store[k - 2]))
                        ma2 = kk.op("vector", lambda e, zsl=zsl, y1sl=y1sl, ln0=ln0: e.tensor_scalar(
                            y1sl, zsl, lncol[:, ln0 + 1:ln0 + 2], lncol[:, ln0 + 3:ln0 + 4],
                            ALU.add, ALU.mult),
                            waits=w_y1, inc=dv)
                        zfree[sl] = (dv, ma2)
                        mv4 = ma2
                        w = [(dv, ma2)]
                        if k >= 2 and y1_store[k - 2]:
                            w.append((qa, y1_store[k - 2]))
                        y1_store[k] = kk.op("scalar", lambda e, k=k, y1sl=y1sl: e.dma_start(
                            out=D["y1d"][k * 128:(k + 1) * 128, :], in_=y1sl), waits=w, inc=qa, k=16)

                        w = [(pe, mmg), (dv, mv4)]
                        if gfree[sl] is not None:
                            w.append(gfree[sl])
                        ma3 = kk.op("vector", lambda e, psgp=psgp, gs=gs, ln0=ln0: e.tensor_scalar(
                            gs[:, 0:8], psgp[:, 0:8], lncol[:, ln0 + 3:ln0 + 4], None, ALU.mult),
                            waits=w, inc=dv)
                        free[psg] = (dv, ma3)
                        SPV()
                        mvb = kk.op("vector", lambda e, ln0=ln0: e.tensor_tensor(
                            lncol[:, ln0 + 8:ln0 + 9], lncol[:, ln0 + 1:ln0 + 2], lncol[:, ln0 + 3:ln0 + 4],
                            ALU.mult), waits=[(dv, ma3)], inc=dv)
                        SPV()
                        mv5 = kk.op("vector", lambda e, gs=gs, ln0=ln0: e.tensor_scalar(
                            gs[:, 8:16], gbcast[:], lncol[:, ln0 + 8:ln0 + 9], None, ALU.mult),
                            waits=[(dv, mvb), (dv, m_setup)], inc=dv)
                        SPV()
                        mv5 = kk.op("vector", lambda e, gs=gs: e.tensor_add(gs[:, 0:8], gs[:, 0:8], gs[:, 8:16]),
                                    waits=[(dv, mv5)], inc=dv)
                        SPV()
                        mv5 = kk.op("vector", lambda e, gs=gs: e.tensor_add(gs[:, 0:8], gs[:, 0:8], bgb[:]),
                                    waits=[(dv, mv5)], inc=dv)
                        SPV()
                        mv6 = kk.op("vector", lambda e, gs=gs, ln0=ln0: e.reduce_max(
                            lncol[:, ln0 + 9:ln0 + 10], gs[:, 0:8], axis=AX.X, negate=True),
                            waits=[(dv, mv5)], inc=dv)
                        SPV()
                        mv6 = kk.op("vector", lambda e, gs=gs, ln0=ln0: e.tensor_scalar(
                            gs[:, 0:8], gs[:, 0:8], lncol[:, ln0 + 9:ln0 + 10], None, ALU.add),
                            waits=[(dv, mv6)], inc=dv)
                        ma4 = kk.op("scalar", lambda e, gs=gs: e.activation(
                            gs[:, 0:8], gs[:, 0:8], AF.Exp),
                            waits=[(dv, mv6)], inc=ac)
                        mv6 = kk.op("vector", lambda e, gs=gs, ln0=ln0: e.reduce_sum(
                            lncol[:, ln0 + 10:ln0 + 11], gs[:, 0:8], axis=AX.X), waits=[(ac, ma4)], inc=dv)
                        SPV()
                        mv6 = kk.op("vector", lambda e, ln0=ln0: e.reciprocal(
                            lncol[:, ln0 + 11:ln0 + 12], lncol[:, ln0 + 10:ln0 + 11]), waits=[(dv, mv6)], inc=dv)
                        pdst = probs_tok[:, :].rearrange("p (e t) -> p e t", t=32)[:, :, k]
                        SPV()
                        mv6 = kk.op("vector", lambda e, gs=gs, pdst=pdst, ln0=ln0: e.tensor_scalar(
                            pdst, gs[:, 0:8], lncol[:, ln0 + 11:ln0 + 12], None, ALU.mult),
                            waits=[(dv, mv6)], inc=dv)
                        lnfree[sl] = (dv, mv6)
                        gfree[sl] = (dv, mv6)
                        psnT = "pS2" if sl == 0 else "pS"
                        w_pf = [(dv, mv6)]
                        if k >= 2 and pf_store[k - 2]:
                            w_pf.append((qa, pf_store[k - 2]))
                        mpf = mm_group(psnT, PS[psnT][0:8, 0:128], [(pdst, ident_f32[:])],
                                       w_pf, "vector",
                                       lambda e, sl=sl, psnT=psnT: e.tensor_copy(
                                           probs_ft[:, sl * 128:(sl + 1) * 128], PS[psnT][0:8, 0:128]), dv)
                        pf_store[k] = kk.op("scalar", lambda e, k=k, sl=sl: e.dma_start(
                            out=D["probsd"][:, k * 128:(k + 1) * 128],
                            in_=probs_ft[:, sl * 128:(sl + 1) * 128]),
                            waits=[(dv, mpf)], inc=qa, k=16)
                m_phaseB = dv.n
                m_probs_store = pf_store[NT - 1]
                ESB.close()
        ESA.close()
        kk.set_gate()

        sc_mark = [0] * NE
        if phases >= 3:
            ESC = contextlib.ExitStack()
            pbcast = ESC.enter_context(nc.sbuf_tensor("sb_"+"pbcast", [128, 8 * T], F32))
            cmpscr = ESC.enter_context(nc.sbuf_tensor("sb_"+"cmpscr", [128, T], F32))
            counts = ESC.enter_context(nc.sbuf_tensor("sb_"+"counts", [128, 8], F32))
            countsT = ESC.enter_context(nc.sbuf_tensor("sb_"+"countsT", [8, 128], F32))
            tauT = ESC.enter_context(nc.sbuf_tensor("sb_"+"tauT", [8, 128], F32))
            tauG = ESC.enter_context(nc.sbuf_tensor("sb_"+"tauG", [128, 8], F32))
            rT = ESC.enter_context(nc.sbuf_tensor("sb_"+"rT", [8, 128 * 3], F32))
            ecol = ESC.enter_context(nc.sbuf_tensor("sb_"+"ecol", [8, 16], F32))
            erow = ESC.enter_context(nc.sbuf_tensor("sb_"+"erow", [1, 16], F32))
            selrow = ESC.enter_context(nc.sbuf_tensor("sb_"+"selrow", [1, 24], F32))
            lobc = ESC.enter_context(nc.sbuf_tensor("sb_lobc", [128, 8], F32))
            dbc = ESC.enter_context(nc.sbuf_tensor("sb_dbc", [128, 8], F32))
            hibc2 = ESC.enter_context(nc.sbuf_tensor("sb_hibc2", [128, 8], F32))
            lobc2 = ESC.enter_context(nc.sbuf_tensor("sb_lobc2", [128, 8], F32))
            rbc2 = ESC.enter_context(nc.sbuf_tensor("sb_rbc2", [128, 8], F32))
            sel_s = ESC.enter_context(nc.sbuf_tensor("sb_"+"sel_s", [128, 256], F32))
            tied_s = ESC.enter_context(nc.sbuf_tensor("sb_"+"tied_s", [128, 256], F32))
            pref_s = ESC.enter_context(nc.sbuf_tensor("sb_"+"pref_s", [128, 256], F32))
            work_s = ESC.enter_context(nc.sbuf_tensor("sb_"+"work_s", [128, 256], F32))
            trow = ESC.enter_context(nc.sbuf_tensor("sb_"+"trow", [1, 256], F32))
            totT = ESC.enter_context(nc.sbuf_tensor("sb_"+"totT", [128, 2], F32))
            offc = ESC.enter_context(nc.sbuf_tensor("sb_"+"offc", [128, 2], F32))
            offrow = ESC.enter_context(nc.sbuf_tensor("sb_"+"offrow", [1, 256], F32))
            aug = ESC.enter_context(nc.sbuf_tensor("sb_"+"aug", [128, 2 * AUGC], BF16))
            if True:
                bc = [0] * NE
                for e0 in range(NE):
                    src = D["probsd"][e0:e0 + 1, :].partition_broadcast(128).rearrange("p o f -> p (o f)")
                    bc[e0] = kk.op("sync", lambda e, e0=e0, src=src: e.dma_start(
                        out=pbcast[:, e0 * T:(e0 + 1) * T], in_=src),
                        waits=[(qa, m_probs_store)], inc=qs, k=16)
                m_bc = qs.n

                mv = kk.op("vector", lambda e: e.memset(ecol[:, 0:1], 0.0), inc=dv)
                mv = kk.op("vector", lambda e: e.memset(ecol[:, 1:2], 1.0), inc=dv)
                mv = kk.op("vector", lambda e: e.memset(ecol[:, 2:3], 0.0), inc=dv)
                m_init = dv.n

                for rnd in range(N_ROUNDS):
                    SPV()
                    mv = kk.op("vector", lambda e: e.tensor_sub(ecol[:, 4:5], ecol[:, 1:2], ecol[:, 0:1]),
                               waits=[(dv, m_init)], inc=dv)
                    SPV()
                    mv = kk.op("vector", lambda e: e.tensor_scalar(
                        tauT[:], grid8[:], ecol[:, 4:5], ecol[:, 0:1], ALU.mult, ALU.add),
                        waits=[(dv, mv)], inc=dv)
                    mvT = mm_group("pS", pS[0:1, 0:8], [(ecol[0:8, 0:1], ident_f32[0:8, 0:8])],
                                   [(dv, mv)], "vector",
                                   lambda e: e.tensor_copy(erow[0:1, 0:8], pS[0:1, 0:8]), dv)
                    mvT = mm_group("pS", pS[0:1, 0:8], [(ecol[0:8, 1:2], ident_f32[0:8, 0:8])],
                                   [(dv, mvT)], "vector",
                                   lambda e: e.tensor_copy(erow[0:1, 8:16], pS[0:1, 0:8]), dv)
                    mvL = mm_group("pS2", pS2[:, 0:8], [(ones_row_f32[:], erow[0:1, 0:8])],
                                   [(dv, mvT)], "vector",
                                   lambda e: e.tensor_copy(lobc[:], pS2[:, 0:8]), dv)
                    mvD = mm_group("pS2", pS2[:, 0:8], [(ones_row_f32[:], erow[0:1, 8:16])],
                                   [(dv, mvL)], "vector",
                                   lambda e: e.tensor_copy(dbc[:], pS2[:, 0:8]), dv)
                    SPV()
                    mvD = kk.op("vector", lambda e: e.tensor_sub(dbc[:], dbc[:], lobc[:]),
                                waits=[(dv, mvD)], inc=dv)
                    SPV()
                    mv = kk.op("vector", lambda e: e.tensor_scalar(
                        tauG[:], dbc[:], grid_col[:, 0:1], None, ALU.mult), waits=[(dv, mvD)], inc=dv)
                    SPV()
                    mv = kk.op("vector", lambda e: e.tensor_add(tauG[:], tauG[:], lobc[:]),
                               waits=[(dv, mv)], inc=dv)
                    m_tau = mv
                    SPV()
                    for e0 in range(NE):
                        mv = kk.op("vector", lambda e, e0=e0: e.tensor_scalar(
                            cmpscr[:], pbcast[:, e0 * T:(e0 + 1) * T], tauG[:, e0:e0 + 1], 0.0,
                            ALU.is_gt, ALU.add, accum_out=counts[:, e0:e0 + 1]),
                            waits=[(dv, m_tau), (qs, bc[e0])], inc=dv)
                    m_cnt = mv
                    mvT = mm_group("pS", pS[0:8, 0:128], [(counts[:], ident_f32[:])],
                                   [(dv, m_cnt)], "vector",
                                   lambda e: e.tensor_copy(countsT[:], pS[0:8, 0:128]), dv)
                    mv = kk.op("vector", lambda e: e.tensor_scalar(
                        rT[:, 0:128], countsT[:], float(KTOP), None, ALU.is_ge), waits=[(dv, mvT)], inc=dv)
                    mv = kk.op("vector", lambda e: e.tensor_scalar(
                        rT[:, 128:256], rT[:, 0:128], -1.0, 1.0, ALU.mult, ALU.add), waits=[(dv, mv)], inc=dv)
                    mv = kk.op("vector", lambda e: e.tensor_scalar(
                        rT[:, 256:384], tauT[:], 1.0, None, ALU.add), waits=[(dv, mv)], inc=dv)
                    mv = kk.op("vector", lambda e: e.tensor_tensor(
                        rT[:, 256:384], rT[:, 256:384], rT[:, 0:128], ALU.mult), waits=[(dv, mv)], inc=dv)
                    mv = kk.op("vector", lambda e: e.reduce_max(ecol[:, 8:9], rT[:, 256:384], axis=AX.X),
                               waits=[(dv, mv)], inc=dv)
                    SPV()
                    mv = kk.op("vector", lambda e: e.tensor_scalar(
                        ecol[:, 8:9], ecol[:, 8:9], -1.0, None, ALU.add), waits=[(dv, mv)], inc=dv)
                    SPV()
                    mv = kk.op("vector", lambda e: e.tensor_max(ecol[:, 0:1], ecol[:, 0:1], ecol[:, 8:9]),
                               waits=[(dv, mv)], inc=dv)
                    mv = kk.op("vector", lambda e: e.tensor_scalar(
                        rT[:, 256:384], tauT[:], -2.0, None, ALU.add), waits=[(dv, mv)], inc=dv)
                    mv = kk.op("vector", lambda e: e.tensor_tensor(
                        rT[:, 256:384], rT[:, 256:384], rT[:, 128:256], ALU.mult), waits=[(dv, mv)], inc=dv)
                    mv = kk.op("vector", lambda e: e.tensor_scalar(
                        rT[:, 256:384], rT[:, 256:384], 2.0, None, ALU.add), waits=[(dv, mv)], inc=dv)
                    mv = kk.op("vector", lambda e: e.tensor_reduce(
                        ecol[:, 1:2], rT[:, 256:384], AX.X, ALU.min), waits=[(dv, mv)], inc=dv)
                    mv = kk.op("vector", lambda e: e.tensor_scalar(
                        rT[:, 256:384], countsT[:], 1.0, None, ALU.add), waits=[(dv, mv)], inc=dv)
                    mv = kk.op("vector", lambda e: e.tensor_tensor(
                        rT[:, 256:384], rT[:, 256:384], rT[:, 128:256], ALU.mult), waits=[(dv, mv)], inc=dv)
                    mv = kk.op("vector", lambda e: e.reduce_max(ecol[:, 2:3], rT[:, 256:384], axis=AX.X),
                               waits=[(dv, mv)], inc=dv)
                    SPV()
                    mv = kk.op("vector", lambda e: e.tensor_scalar(
                        ecol[:, 2:3], ecol[:, 2:3], -1.0, None, ALU.add), waits=[(dv, mv)], inc=dv)
                    m_init = mv

                SPV()
                mv = kk.op("vector", lambda e: e.tensor_scalar(
                    ecol[:, 3:4], ecol[:, 2:3], -1.0, float(KTOP), ALU.mult, ALU.add),
                    waits=[(dv, m_init)], inc=dv)
                mvS = mv
                for ci, col in enumerate((0, 1, 3)):
                    mvS = mm_group("pS", pS[0:1, 0:8], [(ecol[0:8, col:col + 1], ident_f32[0:8, 0:8])],
                                   [(dv, mvS)], "vector",
                                   lambda e, ci=ci: e.tensor_copy(selrow[0:1, ci * 8:(ci + 1) * 8],
                                                                  pS[0:1, 0:8]), dv)
                kk.op("scalar", lambda e: e.dma_start(out=D["thrd"], in_=selrow[:]),
                      waits=[(dv, mvS)], inc=qa, k=16)

                mvB = mm_group("pS", pS[:, 0:8], [(ones_row_f32[:], selrow[0:1, 0:8])],
                               [(dv, mvS)], "vector",
                               lambda e: e.tensor_copy(lobc2[:], pS[:, 0:8]), dv)
                mvB = mm_group("pS", pS[:, 0:8], [(ones_row_f32[:], selrow[0:1, 8:16])],
                               [(dv, mvB)], "vector",
                               lambda e: e.tensor_copy(hibc2[:], pS[:, 0:8]), dv)
                mvB = mm_group("pS", pS[:, 0:8], [(ones_row_f32[:], selrow[0:1, 16:24])],
                               [(dv, mvB)], "vector",
                               lambda e: e.tensor_copy(rbc2[:], pS[:, 0:8]), dv)
                mv = mvB
                for e0 in range(NE):
                    mv = kk.op("vector", lambda e, e0=e0: e.tensor_scalar(
                        sel_s[:, e0 * 32:(e0 + 1) * 32], probs_tok[:, e0 * 32:(e0 + 1) * 32],
                        hibc2[:, e0:e0 + 1], None, ALU.is_gt), waits=[(dv, mv)], inc=dv)
                    mv = kk.op("vector", lambda e, e0=e0: e.tensor_scalar(
                        tied_s[:, e0 * 32:(e0 + 1) * 32], probs_tok[:, e0 * 32:(e0 + 1) * 32],
                        lobc2[:, e0:e0 + 1], None, ALU.is_gt), waits=[(dv, mv)], inc=dv)
                mv = kk.op("vector", lambda e: e.tensor_sub(tied_s[:], tied_s[:], sel_s[:]),
                           waits=[(dv, mv)], inc=dv)
                m_tied = mv

                def prefix256(src_sb, dst_sb, wait_mark):
                    cm1 = mm_group("pB", pB[0:1, 0:256], [(ones_col_f32[:], src_sb[:])],
                                   [(dv, wait_mark)], "vector",
                                   lambda e: e.tensor_copy(trow[:], pB[0:1, 0:256]), dv)
                    cm2 = mm_group("pS", pS[0:128, 0:1], [(trow[0:1, 0:128], ones_row_f32[0:1, 0:1])],
                                   [(dv, cm1)], "vector",
                                   lambda e: e.tensor_copy(totT[:, 0:1], pS[0:128, 0:1]), dv)
                    cm2 = mm_group("pS", pS[0:128, 0:1], [(trow[0:1, 128:256], ones_row_f32[0:1, 0:1])],
                                   [(dv, cm2)], "vector",
                                   lambda e: e.tensor_copy(totT[:, 1:2], pS[0:128, 0:1]), dv)
                    cm3 = mm_group("pB", pB[:, 0:2], [(tril32[:], totT[:])], [(dv, cm2)], "vector",
                                   lambda e: e.tensor_copy(offc[:], pB[:, 0:2]), dv)
                    cm4 = mm_group("pS", pS[0:1, 0:128], [(offc[:, 0:1], ident_f32[:])],
                                   [(dv, cm3)], "vector",
                                   lambda e: e.tensor_copy(offrow[0:1, 0:128], pS[0:1, 0:128]), dv)
                    cm4 = mm_group("pS", pS[0:1, 0:128], [(offc[:, 1:2], ident_f32[:])],
                                   [(dv, cm4)], "vector",
                                   lambda e: e.tensor_copy(offrow[0:1, 128:256], pS[0:1, 0:128]), dv)
                    def f(e, src_sb=src_sb):
                        e.matmul(pA[:, 0:256], trilU[:], src_sb[:], start=True, stop=False)
                        return e.matmul(pA[:, 0:256], ones_row_f32[:], offrow[:], start=False, stop=True)
                    w = [(dv, wait_mark), (dv, cm4)]
                    if free["pA"] is not None:
                        w.append(free["pA"])
                    mmk = kk.op("tensor", f, waits=w, inc=pe)
                    cmk = kk.op("vector", lambda e, dst_sb=dst_sb: e.tensor_copy(dst_sb[:], pA[:, 0:256]),
                                waits=[(pe, mmk)], inc=dv)
                    free["pA"] = (dv, cmk)
                    return cmk

                m_pref = prefix256(tied_s, pref_s, m_tied)
                for e0 in range(NE):
                    mv = kk.op("vector", lambda e, e0=e0: e.tensor_scalar(
                        work_s[:, e0 * 32:(e0 + 1) * 32], pref_s[:, e0 * 32:(e0 + 1) * 32],
                        rbc2[:, e0:e0 + 1], None, ALU.is_le),
                        waits=[(dv, m_pref if e0 == 0 else mv)], inc=dv)
                mv = kk.op("vector", lambda e: e.tensor_mul(work_s[:], work_s[:], tied_s[:]),
                           waits=[(dv, mv)], inc=dv)
                mv = kk.op("vector", lambda e: e.tensor_add(sel_s[:], sel_s[:], work_s[:]),
                           waits=[(dv, mv)], inc=dv)
                for e0 in range(NE):
                    mv = kk.op("vector", lambda e, e0=e0: e.tensor_mul(
                        sel_s[:, e0 * 32:(e0 + 1) * 32], sel_s[:, e0 * 32:(e0 + 1) * 32],
                        halfmask[:]), waits=[(dv, mv)], inc=dv)
                m_selmask = mv
                m_slot = prefix256(sel_s, pref_s, m_selmask)
                mv = kk.op("vector", lambda e: e.tensor_scalar(
                    work_s[:], pref_s[:], float(CAP), None, ALU.is_le), waits=[(dv, m_slot)], inc=dv)
                mv = kk.op("vector", lambda e: e.tensor_mul(work_s[:], work_s[:], sel_s[:]),
                           waits=[(dv, mv)], inc=dv)
                mv = kk.op("vector", lambda e: e.tensor_scalar(
                    tied_s[:], pref_s[:], 1.0, -(1.0 + HUGE), ALU.mult, ALU.add), waits=[(dv, mv)], inc=dv)
                mv = kk.op("vector", lambda e: e.tensor_mul(tied_s[:], tied_s[:], work_s[:]),
                           waits=[(dv, mv)], inc=dv)
                mv = kk.op("vector", lambda e: e.tensor_scalar(
                    tied_s[:], tied_s[:], HUGE, None, ALU.add), waits=[(dv, mv)], inc=dv)
                mv = kk.op("vector", lambda e: e.tensor_copy(inv_i32[:], tied_s[:]),
                           waits=[(dv, mv)], inc=dv)
                m_inv = mv
                kk.op("scalar", lambda e: e.dma_start(out=D["invd"], in_=inv_i32[:]),
                      waits=[(dv, m_inv)], inc=qa, k=16)

                ptokv = probs_tok[:, :].rearrange("p (e t) -> p e t", t=32)
                augz = [0, 0]
                for sl in range(2):
                    augz[sl] = kk.op("vector", lambda e, sl=sl: e.memset(
                        aug[:, sl * AUGC + 1034:sl * AUGC + 1040], 0.0), inc=dv)
                scat_last = [0, 0]
                for k in range(NT):
                    sl = k % 2
                    augsl = aug[:, sl * AUGC:(sl + 1) * AUGC]
                    w = [(qa, y1_store[k]), (dv, augz[sl])]
                    if scat_last[sl]:
                        w.append((qg, scat_last[sl]))
                    yl = kk.op("sync", lambda e, k=k, augsl=augsl: e.dma_start(
                        out=augsl[:, 0:1024], in_=D["y1d"][k * 128:(k + 1) * 128, :]),
                        waits=w, inc=qs, k=16)
                    mv = kk.op("vector", lambda e, augsl=augsl, k=k: e.tensor_copy(
                        augsl[:, 1024:1032], ptokv[:, :, k]), waits=w, inc=dv)
                    mv2 = kk.op("vector", lambda e, augsl=augsl, k=k: e.tensor_copy(
                        augsl[:, 1032:1034], idcols[:, 2 * k:2 * k + 2]), waits=w, inc=dv)
                    last = 0
                    _breg = {}

                    def _get_breg(e):
                        if "r" not in _breg:
                            r = e.alloc_register("bnd639")
                            e.reg_mov(r, 639)
                            _breg["r"] = r
                        return _breg["r"]
                    for e0 in range(NE):
                        last = kk.op("gpsimd", lambda e, e0=e0, k=k, augsl=augsl: e.indirect_dma_start(
                            out=D[f"xind{e0}"], out_offset=IOA(inv_i32[:, e0 * 32 + k:e0 * 32 + k + 1], 0),
                            in_=augsl, in_offset=None,
                            bounds_check=_get_breg(e), oob_is_err=False),
                            waits=[(qs, yl), (dv, mv), (dv, mv2), (dv, m_inv), (qs, m_zfill)],
                            inc=qg, k=16)
                        sc_mark[e0] = last
                    scat_last[sl] = last

            ESC.close()
            kk.set_gate()
        m_scadd = 0
        if phases >= 4:
            ESD = contextlib.ExitStack()
            w1buf = ESD.enter_context(nc.sbuf_tensor("sb_"+"w1buf", [128, 2 * 4096], BF16))
            w2full = ESD.enter_context(nc.sbuf_tensor("sb_"+"w2full", [128, 32 * C], BF16))
            xint = ESD.enter_context(nc.sbuf_tensor("sb_"+"xint", [128, 5 * AUGC], BF16))
            xinf = ESD.enter_context(nc.sbuf_tensor("sb_"+"xinf", [128, 8 * 640], BF16))
            hsb = ESD.enter_context(nc.sbuf_tensor("sb_"+"hsb", [128, 32 * 576], BF16))
            yet = ESD.enter_context(nc.sbuf_tensor("sb_"+"yet", [128, 2 * C], F32))
            pslotc = ESD.enter_context(nc.sbuf_tensor("sb_"+"pslotc", [128, 8], F32))
            fwdf = ESD.enter_context(nc.sbuf_tensor("sb_"+"fwdf", [128, 16], F32))
            fwdI = ESD.enter_context(nc.sbuf_tensor("sb_"+"fwdI", [128, 8], I32))
            if True:
                xiv = xint[:, :].rearrange("p (st c) -> p st c", c=AUGC)
                m_xinf_prev = 0
                m_hs_prev = 0
                yet_free = [None, None]
                w1_free = [None, None]
                w2_ld = [0] * 8
                for e0 in range(NE):
                    w = [(qg, sc_mark[e0] if phases >= 3 else 0)]
                    if m_xinf_prev:
                        w.append((pe, m_xinf_prev))
                    mld = kk.op("sync", lambda e, e0=e0: e.dma_start(
                        out=xint[:].rearrange("p (st c) -> p st c", c=AUGC),
                        in_=D[f"xind{e0}"].rearrange("(st p) c -> p st c", p=128)),
                        waits=w, inc=qs, k=16)
                    for b in range(8):
                        w = [(pe, m_hs_prev)] if m_hs_prev else []
                        w2_ld[b] = kk.op("gpsimd", lambda e, e0=e0, b=b: e.dma_start(
                            out=w2full[:, b * 4096:(b + 1) * 4096].rearrange("p (jj c) -> p jj c", c=C),
                            in_=D["w2"][e0][b * 512:(b + 1) * 512, :].rearrange("(jj p) c -> p jj c", p=128)),
                            waits=w, inc=qg, k=16)
                    m_w2 = qg.n
                    last = 0
                    for st in range(5):
                        for cc in range(8):
                            psn = "pT" if (st * 8 + cc) % 2 == 0 else "pT2"
                            last = transp(psn, xiv[:, st, cc * 128:(cc + 1) * 128],
                                          lambda e, oap, cc=cc, st=st: e.tensor_copy(
                                              xinf[:, cc * 640 + st * 128: cc * 640 + st * 128 + 128], oap),
                                          ("vector", dv), waits=[(qs, mld)])
                    m_xinf = last
                    ma = kk.op("scalar", lambda e, e0=e0: e.activation(
                        pslotc[:, 0:5], xiv[:, :, 1024 + e0], AF.Copy), waits=[(qs, mld)], inc=ac)
                    m_pslot = ma
                    mv = kk.op("vector", lambda e: e.tensor_scalar(
                        fwdf[:, 0:5], xiv[:, :, 1032], 32.0, None, ALU.mult), waits=[(qs, mld)], inc=dv)
                    SPV()
                    mv = kk.op("vector", lambda e: e.tensor_tensor(
                        fwdf[:, 8:13], fwdf[:, 0:5], xiv[:, :, 1033], ALU.add), waits=[(dv, mv)], inc=dv)
                    SPV()
                    mv = kk.op("vector", lambda e: e.tensor_copy(fwdI[:, 0:5], fwdf[:, 8:13]),
                               waits=[(dv, mv)], inc=dv)
                    m_fwd = mv

                    h_ev = 0
                    for j in range(HEXP // 128):
                        b = j // 4
                        sl = b % 2
                        if j % 4 == 0:
                            w = [(pe, w1_free[sl][1])] if w1_free[sl] else []
                            w1ld = kk.op("gpsimd", lambda e, e0=e0, b=b, sl=sl: e.dma_start(
                                out=w1buf[:, sl * 4096:(sl + 1) * 4096].rearrange("p (cc h) -> p cc h", h=512),
                                in_=D["w1"][e0][:, b * 512:(b + 1) * 512].rearrange("(cc p) h -> p cc h", p=128)),
                                waits=w, inc=qg, k=16)
                            cur_w1ld = w1ld
                        psn512 = "pA" if j % 2 == 0 else "pB"
                        psn64 = "pS" if j % 2 == 0 else "pS2"
                        for part, psn, n0, n1 in ((0, psn512, 0, 512), (1, psn64, 512, 576)):
                            ps_ap = PS[psn][:] if part == 0 else PS[psn][:, 0:64]
                            pairs = []
                            for cc in range(8):
                                lh = w1buf[:, sl * 4096 + cc * 512 + (j % 4) * 128:
                                           sl * 4096 + cc * 512 + (j % 4) * 128 + 128]
                                rh = xinf[:, cc * 640 + n0:cc * 640 + n1]
                                pairs.append((lh, rh))
                            wdt = n1 - n0
                            h_ev = mm_group(psn, ps_ap, pairs, [(qg, cur_w1ld), (dv, m_xinf)],
                                            "scalar",
                                            lambda e, j=j, n0=n0, wdt=wdt, psn=psn, part=part: e.activation(
                                                hsb[:, j * 576 + n0:j * 576 + n0 + wdt],
                                                PS[psn][:] if part == 0 else PS[psn][:, 0:64],
                                                AF.Gelu), ac)
                        if j % 4 == 3:
                            w1_free[sl] = (pe, pe.n)
                    m_h = h_ev
                    m_xinf_prev = pe.n

                    for st in range(5):
                        mwidth = 128 if st < 4 else 64
                        for ch in range(2):
                            psn = ("pC", "pD")[(st * 2 + ch) % 2]
                            ps = PS[psn]
                            pairs = []
                            for j in range(32):
                                lh = hsb[:, j * 576 + st * 128: j * 576 + st * 128 + mwidth]
                                rh = w2full[:, j * 1024 + ch * 512: j * 1024 + ch * 512 + 512]
                                pairs.append((lh, rh))
                            sl = st % 2
                            def ev(e, ps=ps, sl=sl, ch=ch, st=st, mwidth=mwidth):
                                return e.activation(
                                    yet[0:mwidth, sl * C + ch * 512: sl * C + ch * 512 + 512],
                                    ps[0:mwidth, :], AF.Copy, scale=pslotc[0:mwidth, st:st + 1])
                            w = [(ac, m_h), (qg, m_w2), (ac, m_pslot)]
                            if yet_free[sl] is not None:
                                w.append(yet_free[sl])
                            cmk = mm_group(psn, ps[0:mwidth, :], pairs, w, "scalar", ev, ac)
                        sa = kk.op("gpsimd", lambda e, sl=sl, st=st, mwidth=mwidth: e.indirect_dma_start(
                            out=D["accd"], out_offset=IOA(fwdI[0:mwidth, st:st + 1], 0),
                            in_=yet[0:mwidth, sl * C:sl * C + C], in_offset=None,
                            compute_op=ALU.add),
                            waits=[(ac, cmk), (dv, m_fwd), (qs, m_zfill)], inc=qg, k=16)
                        yet_free[sl] = (qg, sa)
                        m_scadd = sa
                    m_hs_prev = pe.n

            ESD.close()
            kk.set_gate()
        if phases >= 5:
            ESE = contextlib.ExitStack()
            acct = ESE.enter_context(nc.sbuf_tensor("sb_"+"acct", [128, 2 * C], F32))
            y1e = ESE.enter_context(nc.sbuf_tensor("sb_"+"y1e", [128, 2 * C], F32))
            ot = ESE.enter_context(nc.sbuf_tensor("sb_"+"ot", [128, 2 * C], F32))
            if True:
                lnfree = [None, None]
                ofree = [None, None]
                for k in range(NT):
                    sl = k % 2
                    ln0 = sl * 12
                    asl = acct[:, sl * C:sl * C + C]
                    ysl = y1e[:, sl * C:sl * C + C]
                    osl = ot[:, sl * C:sl * C + C]
                    w = [(qg, m_scadd)]
                    if ofree[sl] is not None:
                        w.append(ofree[sl])
                    al = kk.op("sync", lambda e, k=k, asl=asl: e.dma_start(
                        out=asl, in_=D["accd"][k * 128:(k + 1) * 128, :]), waits=w, inc=qs, k=16)
                    yl = kk.op("gpsimd", lambda e, k=k, ysl=ysl: e.dma_start(
                        out=ysl, in_=D["y1d"][k * 128:(k + 1) * 128, :]),
                        waits=w + [(qa, y1_store[k] if phases >= 2 else 0)], inc=qg, k=16)
                    mz = kk.op("vector", lambda e, asl=asl, ysl=ysl: e.tensor_add(asl, asl, ysl),
                               waits=[(qs, al), (qg, yl)], inc=dv)
                    w = [(dv, mz)]
                    if lnfree[sl] is not None:
                        w.append(lnfree[sl])
                    mv = kk.op("vector", lambda e, asl=asl, ln0=ln0: e.reduce_sum(
                        lncol[:, ln0:ln0 + 1], asl, axis=AX.X, negate=True), waits=w, inc=dv)
                    SPV()
                    mv = kk.op("vector", lambda e, ln0=ln0: e.tensor_scalar_mul(
                        lncol[:, ln0 + 1:ln0 + 2], lncol[:, ln0:ln0 + 1], 1.0 / C), waits=[(dv, mv)], inc=dv)
                    mv2 = kk.op("vector", lambda e, asl=asl, ln0=ln0: e.scalar_tensor_tensor(
                        sqscr[:], asl, 1.0, asl, ALU.mult, ALU.mult, accum_out=lncol[:, ln0 + 2:ln0 + 3]),
                        waits=[(dv, mv)], inc=dv)
                    mv3 = kk.op("vector", lambda e, ln0=ln0: e.tensor_tensor(
                        lncol[:, ln0 + 3:ln0 + 4], lncol[:, ln0 + 1:ln0 + 2], lncol[:, ln0 + 1:ln0 + 2],
                        ALU.mult), waits=[(dv, mv2)], inc=dv)
                    mv3 = kk.op("vector", lambda e, ln0=ln0: e.tensor_scalar(
                        lncol[:, ln0 + 4:ln0 + 5], lncol[:, ln0 + 2:ln0 + 3], 1.0 / C, None, ALU.mult),
                        waits=[(dv, mv3)], inc=dv)
                    SPV()
                    mv3 = kk.op("vector", lambda e, ln0=ln0: e.tensor_sub(
                        lncol[:, ln0 + 5:ln0 + 6], lncol[:, ln0 + 4:ln0 + 5], lncol[:, ln0 + 3:ln0 + 4]),
                        waits=[(dv, mv3)], inc=dv)
                    ma = kk.op("scalar", lambda e, ln0=ln0: e.activation(
                        lncol[:, ln0 + 6:ln0 + 7], lncol[:, ln0 + 5:ln0 + 6], AF.Ln, bias=epscol[:]),
                        waits=[(dv, mv3)], inc=ac)
                    SPA()
                    ma = kk.op("scalar", lambda e, ln0=ln0: e.activation(
                        lncol[:, ln0 + 7:ln0 + 8], lncol[:, ln0 + 6:ln0 + 7], AF.Exp, scale=-0.5),
                        waits=[(ac, ma)], inc=ac)
                    mv4 = kk.op("vector", lambda e, ln0=ln0: e.tensor_copy(
                        lncol[:, ln0 + 3:ln0 + 4], lncol[:, ln0 + 7:ln0 + 8]), waits=[(ac, ma)], inc=dv)
                    SPV()
                    w_ot = [(dv, mv4)]
                    if ofree[sl] is not None:
                        w_ot.append(ofree[sl])
                    ma2 = kk.op("vector", lambda e, asl=asl, osl=osl, ln0=ln0: e.tensor_scalar(
                        osl, asl, lncol[:, ln0 + 1:ln0 + 2], lncol[:, ln0 + 3:ln0 + 4],
                        ALU.add, ALU.mult),
                        waits=w_ot, inc=dv)
                    lnfree[sl] = (dv, ma2)
                    ost = kk.op("scalar", lambda e, k=k, osl=osl: e.dma_start(
                        out=D["out"][k * 128:(k + 1) * 128, :], in_=osl), waits=[(dv, ma2)], inc=qa, k=16)
                    ofree[sl] = (qa, ost)

        kk.op("sync", lambda e: e.wait_ge(qa.sem, qa.n) or None)
        kk.op("sync", lambda e: e.wait_ge(qg.sem, qg.n) or None)

        kk.emit(block)
    return nc


def kernel(L0_attn_x, L0_attn_Wk, L0_attn_Wq, L0_attn_Wv, L0_attn_W_out, L0_attn_b_out,
           L0_ffn_gate_w, L0_ffn_w1, L0_ffn_w2):
    nc = build_nc(phases=5, debug=False)
    consts = [make_consts(g) for g in range(2)]
    in_maps = []
    for c in range(8):
        b, g = c // 2, c % 2
        m = {
            "x": np.ascontiguousarray(np.asarray(L0_attn_x)[b]),
            "wk": np.ascontiguousarray(np.asarray(L0_attn_Wk)),
            "wq": np.ascontiguousarray(np.asarray(L0_attn_Wq)),
            "wv": np.ascontiguousarray(np.asarray(L0_attn_Wv)),
            "wout": np.ascontiguousarray(np.asarray(L0_attn_W_out)),
            "bout": np.ascontiguousarray(np.asarray(L0_attn_b_out).reshape(1, C)),
            "gatew": np.ascontiguousarray(np.asarray(L0_ffn_gate_w)),
            "w1": np.ascontiguousarray(np.asarray(L0_ffn_w1)),
            "w2": np.ascontiguousarray(np.asarray(L0_ffn_w2)),
        }
        m.update(consts[g])
        in_maps.append(m)
    res = run_bass_kernel_spmd(nc, in_maps, core_ids=list(range(8)))
    out = np.empty((4, T, C), np.float32)
    for b in range(4):
        out[b, :2048] = res.results[2 * b]["out"][:2048]
        out[b, 2048:] = res.results[2 * b + 1]["out"][2048:]
    return out
